# version 17 (final)
# baseline (speedup 1.0000x reference)
import numpy as np
import ml_dtypes
import concourse.bass as bass
import concourse.mybir as mybir
from concourse.bass_utils import run_bass_kernel_spmd
from concourse.bass_core_rust import IndirectOffsetOnAxis
from contextlib import ExitStack

F32 = mybir.dt.float32
BF16 = mybir.dt.bfloat16
I32 = mybir.dt.int32
U32 = mybir.dt.uint32
AF = mybir.ActivationFunctionType
ALU = mybir.AluOpType
AXX = mybir.AxisListType.X

MAGIC = 12582912.0
LN_EPS = 1e-5
D = 4096
NT = 1024
NKV = 2048
HD = 128
NF = 2176
NFC = 17
QB = 127.0
RSQ = float(1.0 / np.sqrt(128.0))
NEGINF = -1.0e30

PHASES = ("tern", "quant", "qkv", "attn", "ffn1", "attnb", "ffn2")


class B:
    ENGS = ("sync", "tensor", "vector", "scalar", "gpsimd")

    def __init__(self, nc, ctx):
        self.nc = nc
        self.ctx = ctx
        self.q = {e: [] for e in self.ENGS}
        self.sems = {}
        self.cnt = {}
        self.last_wait = {}

    def sem(self, name):
        if name not in self.sems:
            self.sems[name] = self.ctx.enter_context(self.nc.semaphore(name))
            self.cnt[name] = 0
        return self.sems[name]

    def c(self, name):
        self.sem(name)
        return self.cnt[name]

    def on(self, eng, fn, waits=(), inc=None, inc_n=1):
        if inc is not None:
            self.sem(inc)
            self.cnt[inc] += inc_n
        for s, _ in waits:
            self.sem(s)
        waits = tuple(waits)

        def task(e, fn=fn, waits=waits, inc=inc, inc_n=inc_n, eng=eng):
            for s, v in waits:
                k = (eng, s)
                if self.last_wait.get(k, -1) >= v:
                    continue
                self.last_wait[k] = v
                e.wait_ge(self.sems[s], v)
            r = fn(e)
            if inc is not None:
                r.then_inc(self.sems[inc], inc_n)
        self.q[eng].append(task)
        return self.cnt[inc] if inc is not None else None

    def dma(self, eng, out, in_, waits=(), **kw):
        return self.on(eng, lambda e: e.dma_start(out=out, in_=in_, **kw),
                       waits=waits, inc=eng + "dma", inc_n=16)

    def barrier(self):
        self.sem("bar")
        for eng in self.ENGS:
            self.on(eng, lambda e: e.nop(), inc="bar")
        target = self.cnt["bar"]
        for eng in self.ENGS:
            def w(e, t=target, eng=eng):
                self.last_wait[(eng, "bar")] = t
                e.wait_ge(self.sems["bar"], t)
                return e.nop()
            self.q[eng].append(w)
        return target

    def emit(self, block):
        for eng in self.ENGS:
            tasks = self.q[eng]
            if not tasks:
                continue

            def body(e, tasks=tasks):
                for t in tasks:
                    t(e)
            getattr(block, eng)(body)


def build_program(dbg=()):
    nc = bass.Bass("TRN2", target_bir_lowering=False, debug=False)
    dt_ = nc.dram_tensor

    x_own = dt_("x_own", [NT, D], F32, kind="ExternalInput")
    g_h = dt_("gamma1", [1, D], F32, kind="ExternalInput")
    bta_h = dt_("beta1", [1, D], F32, kind="ExternalInput")
    wq_sh = dt_("wq_sh", [512, D], F32, kind="ExternalInput").ap()
    wk_sh = dt_("wk_sh", [128, D], F32, kind="ExternalInput").ap()
    wv_sh = dt_("wv_sh", [128, D], F32, kind="ExternalInput").ap()
    wo_sh = dt_("wo_sh", [512, D], F32, kind="ExternalInput").ap()
    up_h = dt_("up_w", [16, 256, 1024], F32, kind="ExternalInput")
    dn_h = dt_("down_w", [16, 1024, 256], F32, kind="ExternalInput")
    perm_h = dt_("perm_in", [D], I32, kind="ExternalInput")
    cos_ch = dt_("cos_ch", [NFC, D, 128], BF16, kind="ExternalInput")
    sin_ch = dt_("sin_ch", [NFC, D, 128], BF16, kind="ExternalInput")
    icos_ch = dt_("icos_ch", [32, NF, 128], BF16, kind="ExternalInput")
    isin_ch = dt_("isin_ch", [32, NF, 128], BF16, kind="ExternalInput")
    cosrow_h = dt_("cos_rows", [D, NF], BF16, kind="ExternalInput")
    rc_h = dt_("ropecos", [128, NKV], F32, kind="ExternalInput")
    rs_h = dt_("ropesin", [128, NKV], F32, kind="ExternalInput")
    rco_h = dt_("ropecos_own", [128, NT], F32, kind="ExternalInput")
    rso_h = dt_("ropesin_own", [128, NT], F32, kind="ExternalInput")
    out_h = dt_("out", [NT, D], F32, kind="ExternalOutput")
    out = out_h.ap()

    meanp_in = dt_("meanp_in", [128, 24], F32)
    meanp_out = dt_("meanp_out", [128, 24], F32)
    meanrow = dt_("meanrow", [3072], F32)
    scal_hbm = dt_("scal_hbm", [16], F32)
    tq_in = dt_("tq_in", [D, 512], BF16)
    tk_in = dt_("tk_in", [D, 128], BF16)
    tv_in = dt_("tv_in", [D, 128], BF16)
    to_in = dt_("to_in", [D, 512], BF16)
    tq = dt_("tq", [8, D, 512], BF16)
    tk = dt_("tk", [8, D, 128], BF16)
    tv = dt_("tv", [8, D, 128], BF16)
    to = dt_("to", [8, D, 512], BF16)
    kv_in = dt_("kv_in", [2048, 1024], F32)
    kv = dt_("kv", [2, 2048, 1024], F32)
    qT_sp = dt_("qT_sp", [D, NT], F32)
    att_sp = dt_("att_sp", [NT, D], BF16)
    sums_sp = dt_("sums_sp", [32 * NT], F32)
    res_a = dt_("res_a", [NT, D], F32)
    res_b = dt_("res_b", [NT, D], F32)
    fg_a = dt_("fg_a", [D, NT], BF16)
    fg_b = dt_("fg_b", [D, NT], BF16)
    xr_sp = dt_("xr_sp", [NF, NT], BF16)
    xi_sp = dt_("xi_sp", [NF, NT], BF16)
    corr_sp = dt_("corr_sp", [NT, NF], F32)

    dbg_aps = {}
    ctx = ExitStack()
    b = B(nc, ctx)
    sb = lambda name, shape, dt: ctx.enter_context(nc.sbuf_tensor(name, shape, dt))

    def dbg_out(name, shape, dt):
        if name in dbg:
            dbg_aps[name] = dt_("dbg_" + name, list(shape), dt, kind="ExternalOutput")
            return dbg_aps[name].ap()
        return None

    arA = sb("arA", [128, 32768], BF16)
    arB = sb("arB", [128, 18432], BF16)
    stage = sb("stage", [128, 4096], F32)
    stage2 = sb("stage2", [128, 4096], F32)
    bstage = sb("bstage", [128, 4096], BF16)
    evbuf = sb("evbuf", [128, 4096], BF16)
    tbuf = sb("tbuf", [128, 8192], BF16)
    wbuf = sb("wbuf", [128, 2048], F32)
    rcos = sb("rcos", [128, NKV], F32)
    rsin = sb("rsin", [128, NKV], F32)
    rco = sb("rco", [128, NT], F32)
    rso = sb("rso", [128, NT], F32)
    gam = arB[:, 0:D]
    bet = arB[:, D:2 * D]
    small = sb("small", [128, 48], F32)
    scal = sb("scal", [128, 16], F32)
    sumscol = sb("sumscol", [128, 256], F32)
    bn48 = sb("bn48", [128, 48], F32)
    idxs = sb("idxs", [128, 32], I32)
    topk = sb("topk", [128, 64], F32)
    topkb = sb("topkb", [128, 8], F32)
    ixu = sb("ixu", [128, 8], U32)
    ident = sb("ident", [128, 128], BF16)
    onescol = sb("onescol", [128, 2], BF16)
    onesrow = sb("onesrow", [1, 128], BF16)
    srow = sb("srow", [1, 1296], F32)
    kvrow = sb("kvrow", [1, 16], F32)
    junk = sb("junk", [128, 16], F32)
    junkr = sb("junkr", [1, 16], F32)
    junku = sb("junku", [128, 8], U32)
    junkb = sb("junkb", [128, 8], BF16)
    cst = sb("cst", [128, 2], F32)
    ps = [ctx.enter_context(nc.psum_tensor(f"ps{i}", [128, 512], F32)) for i in range(8)]

    def TT(e, o, a, c, op):
        return e.tensor_tensor(out=o, in0=a, in1=c, op=op)

    def PAD(e, n=5):
        for _ in range(n):
            e.tensor_copy(out=junk[:, 12:16], in_=junk[:, 0:4])

    def TSS(e, o, a, s, op):
        return e.tensor_single_scalar(out=o, in_=a, scalar=s, op=op)

    b.on("gpsimd", lambda e: e.iota(stage[:, 0:128], [[1, 128]], base=0,
                                    channel_multiplier=-1,
                                    allow_small_or_imprecise_dtypes=True), inc="g")
    gi = b.c("g")
    b.on("vector", lambda e: TSS(e, ident[:], stage[:, 0:128], 0.0, ALU.is_equal),
         waits=[("g", gi)], inc="v")
    b.on("vector", lambda e: e.memset(onescol[:], 1.0), inc="v")
    b.on("vector", lambda e: e.memset(onesrow[:], 1.0), inc="v")
    b.on("vector", lambda e: e.memset(cst[:, 0:1], LN_EPS), inc="v")
    b.on("vector", lambda e: e.memset(cst[:, 1:2], MAGIC), inc="v")
    v_init = b.c("v")
    b.dma("gpsimd", gam[:], bass.AP(g_h, 0, [[0, 128], [1, D]]))
    b.dma("gpsimd", bet[:], bass.AP(bta_h, 0, [[0, 128], [1, D]]))
    b.dma("sync", rcos[:], rc_h.ap()[:, :])
    b.dma("sync", rsin[:], rs_h.ap()[:, :])
    b.dma("sync", rco[:], rco_h.ap()[:, :])
    b.dma("sync", rso[:], rso_h.ap()[:, :])
    b.dma("sync", idxs[:, 0:32], perm_h.ap().rearrange("(a c) -> c a", c=128),
          allow_slow_non_contiguous=True)
    consts_g = b.c("gpsimddma")

    shards = [(wq_sh, 4, tq_in, 512), (wk_sh, 1, tk_in, 128),
              (wv_sh, 1, tv_in, 128), (wo_sh, 4, to_in, 512)]
    NUMEL = [D * D, 1024 * D, 1024 * D, D * D]

    tern_ok = 0
    if "tern" in PHASES:
        b.on("vector", lambda e: e.memset(small[:, 0:24], 0.0), inc="v")
        nb = 0
        for ap, nblk, _, _ in shards:
            for blk in range(nblk):
                for hf in range(2):
                    lw = b.dma("sync", wbuf[:],
                               ap[blk * 128:(blk + 1) * 128, hf * 2048:(hf + 1) * 2048],
                               waits=[("s", b.c("s"))])
                    b.on("scalar", lambda e, c=nb + 10 * hf: e.activation(
                            out=stage2[:, 0:2048], in_=wbuf[:], func=AF.Abs,
                            accum_out=small[:, c:c + 1]),
                         waits=[("syncdma", lw)], inc="s")
                nb += 1
        sdn = b.c("s")
        def addhalves(e):
            e.tensor_add(out=small[:, 24:34], in0=small[:, 0:10], in1=small[:, 10:20])
            e.memset(small[:, 34:48], 0.0)
            PAD(e)
            return e.tensor_copy(out=junk[:, 0:10], in_=small[:, 24:34])
        b.on("vector", addhalves, waits=[("s", sdn)], inc="v")
        b.dma("gpsimd", meanp_in.ap()[:, 0:24], small[:, 24:48],
              waits=[("v", b.c("v"))])
        b.on("gpsimd", lambda e: e.collective_compute(
                "AllReduce", ALU.add, replica_groups=[list(range(8))],
                ins=[meanp_in.ap().opt()], outs=[meanp_out.ap().opt()]),
             waits=[("gpsimddma", b.c("gpsimddma"))], inc="cc")
        lw = b.dma("sync", stage[:, 0:3072],
                   bass.AP(meanp_out, 0, [[0, 128], [1, 3072]]),
                   waits=[("cc", b.c("cc"))])
        stv = stage[:, 0:3072].rearrange("p (pp c) -> p pp c", c=24)

        def theta_calc(e):
            for m, (lo, hi) in enumerate([(0, 4), (4, 5), (5, 6), (6, 10)]):
                e.tensor_reduce(out=stage2[:, m * 128:(m + 1) * 128],
                                in_=stv[:, :, lo:hi], axis=AXX, op=ALU.add)
            for m in range(4):
                e.tensor_reduce(out=small[:, 36 + m:37 + m],
                                in_=stage2[:, m * 128:(m + 1) * 128],
                                axis=AXX, op=ALU.add)
            for m in range(4):
                TSS(e, scal[:, 8 + m:9 + m], small[:, 36 + m:37 + m],
                    1.0 / NUMEL[m], ALU.mult)
            PAD(e)
            e.tensor_scalar_max(scal[:, 0:4], scal[:, 8:12], 1e-5)
            PAD(e)
            for _ in range(2):
                for m in range(4):
                    e.reciprocal(out=scal[:, 12 + m:13 + m], in_=scal[:, 0 + m:1 + m])
            PAD(e)
            TSS(e, scal[:, 4:8], scal[:, 12:16], 2.0, ALU.mult)
            PAD(e)
            return e.tensor_copy(out=junk[:, 8:12], in_=scal[:, 4:8])
        b.on("vector", theta_calc, waits=[("syncdma", lw)], inc="v")
        d = dbg_out("partials", (128, 24), F32)
        if d is not None:
            b.dma("sync", d[:, :], small[:, 0:24], waits=[("v", b.c("v"))])
        d = dbg_out("scal", (128, 16), F32)
        if d is not None:
            b.dma("sync", d[:, :], scal[:, 0:16], waits=[("v", b.c("v"))])
        means_ok = b.c("v")

        for mi, (ap, nblk, t_in, cols) in enumerate(shards):
            for blk in range(nblk):
                for hf in range(2):
                    lw = b.dma("scalar", wbuf[:],
                               ap[blk * 128:(blk + 1) * 128, hf * 2048:(hf + 1) * 2048],
                               waits=[("v", means_ok), ("s", b.c("s")),
                                      ("v", b.c("v"))])
                    sl = slice(hf * 2048, (hf + 1) * 2048)
                    b.on("scalar", lambda e, mi=mi, sl=sl: e.activation(
                            out=stage[:, sl], in_=wbuf[:], func=AF.Abs,
                            scale=scal[:, 4 + mi:5 + mi]),
                         waits=[("scalardma", lw)], inc="s")
                    b.on("scalar", lambda e, sl=sl: e.sign(out=stage2[:, sl], in_=wbuf[:]),
                         inc="s")
                sdn = b.c("s")
                b.on("vector", lambda e: e.scalar_tensor_tensor(
                        out=bstage[:], in0=stage[:], scalar=1.0, in1=stage2[:],
                        op0=ALU.is_ge, op1=ALU.mult),
                     waits=[("s", sdn)], inc="v")
                vdn = b.c("v")
                for grp in range(4):
                    bank = grp % 2
                    for j in range(8):
                        blkid = grp * 8 + j
                        b.on("tensor", lambda e, bank=bank, j=j, blkid=blkid: e.transpose(
                                ps[bank][:, j * 64:(j + 1) * 64].bitcast(BF16),
                                bstage[:, blkid * 128:(blkid + 1) * 128], ident[:]),
                             waits=[("v", vdn), ("v", v_init)],
                             inc=("t" if j == 7 else None))
                    tdn = b.c("t")
                    b.on("vector", lambda e, bank=bank, grp=grp: e.tensor_copy(
                            out=arA[:, grp * 1024:(grp + 1) * 1024],
                            in_=ps[bank][:, 0:512].bitcast(BF16)),
                         waits=[("t", tdn)], inc="v")
                vdn2 = b.c("v")
                b.dma("gpsimd",
                      bass.AP(t_in, blk * 128,
                              [[cols, 128], [128 * cols, 32], [1, 128]]),
                      arA[:, 0:4096], waits=[("v", vdn2)])
                b.on("vector", lambda e: e.nop(),
                     waits=[("gpsimddma", b.c("gpsimddma"))], inc="v")
        gall = b.c("gpsimddma")
        cc_per = []
        for (ap, _, t_in, _), t_out in zip(shards, [tq, tk, tv, to]):
            b.on("gpsimd", lambda e, t_in=t_in, t_out=t_out: e.collective_compute(
                    "AllGather", ALU.bypass, replica_groups=[list(range(8))],
                    ins=[t_in.ap().opt()], outs=[t_out.ap().opt()]),
                 waits=[("gpsimddma", gall)], inc="cc")
            cc_per.append(b.c("cc"))
        cc_q, cc_k, cc_v, cc_o = cc_per
        tern_ok = b.c("cc")

        d = dbg_out("ternq", (8, D, 512), BF16)
        if d is not None:
            b.dma("sync", d[:, :, :], tq.ap()[:, :, :], waits=[("cc", tern_ok)])

    b.barrier()

    def ln_stats(e, src4096):
        for s8 in range(8):
            e.bn_stats(out=bn48[:, s8 * 6:(s8 + 1) * 6],
                       in_=src4096[:, s8 * 512:(s8 + 1) * 512])
        return e.bn_aggr(out=small[:, 40:42], in_=bn48[:, 0:48])

    def ln_coeffs(rcol, bcol):
        b.on("scalar", lambda e: e.activation(out=small[:, 42:43], in_=small[:, 41:42],
                                              func=AF.Sqrt, bias=cst[:, 0:1]),
             waits=[("v", b.c("v"))], inc="s")
        sdn = b.c("s")

        def fin(e):
            e.reciprocal(out=small[:, rcol:rcol + 1], in_=small[:, 42:43])
            e.reciprocal(out=small[:, rcol:rcol + 1], in_=small[:, 42:43])
            PAD(e)
            e.scalar_tensor_tensor(out=small[:, bcol:bcol + 1],
                                   in0=small[:, 40:41], scalar=-1.0,
                                   in1=small[:, rcol:rcol + 1],
                                   op0=ALU.mult, op1=ALU.mult)
            PAD(e)
            return e.tensor_copy(out=junk[:, 0:2], in_=small[:, rcol:rcol + 2])
        b.on("vector", fin, waits=[("s", sdn)], inc="v")
        return b.c("v")

    tr_evac = {}

    def transpose_to_arena(src_bf16, arena, chunk_cols, col_off, banks, deps, nblk=32):
        for grp in range((nblk + 7) // 8):
            bank = banks[grp % 2]
            j0 = grp * 8
            jn = min(8, nblk - j0)
            for j in range(jn):
                blkid = j0 + j
                wl = list(deps) + [("v", v_init)]
                key = ("tr", bank)
                if key in tr_evac:
                    wl.append(("v", tr_evac[key]))
                b.on("tensor", lambda e, bank=bank, j=j, blkid=blkid: e.transpose(
                        ps[bank][:, j * 64:(j + 1) * 64].bitcast(BF16),
                        src_bf16[:, blkid * 128:(blkid + 1) * 128], ident[:]),
                     waits=wl, inc=("t" if j == jn - 1 else None))
            tdn = b.c("t")
            dst = arena[:].rearrange("p (c t) -> p c t", t=chunk_cols)[
                :, j0:j0 + jn, col_off:col_off + 128]
            b.on("vector", lambda e, dst=dst, bank=bank, jn=jn: e.tensor_copy(
                    out=dst, in_=ps[bank][:, 0:jn * 64].bitcast(BF16)),
                 waits=[("t", tdn)], inc="v")
            tr_evac[("tr", bank)] = b.c("v")
        return b.c("v")

    if "quant" in PHASES:
        for tt in range(8):
            lw = b.dma("sync", stage[:], x_own.ap()[tt * 128:(tt + 1) * 128, :],
                       waits=[("v", b.c("v")), ("s", b.c("s"))])
            b.on("vector", lambda e: ln_stats(e, stage),
                 waits=[("syncdma", lw)], inc="v")
            ln_coeffs(2, 3)
            b.on("scalar", lambda e: e.activation(out=stage2[:], in_=stage[:], func=AF.Identity,
                                                  scale=small[:, 2:3], bias=small[:, 3:4]),
                 waits=[("v", b.c("v"))], inc="s")
            b.on("vector", lambda e: ln_stats(e, stage2),
                 waits=[("s", b.c("s"))], inc="v")
            ln_coeffs(6, 7)
            b.on("scalar", lambda e: e.activation(out=stage2[:], in_=stage2[:], func=AF.Identity,
                                                  scale=small[:, 6:7], bias=small[:, 7:8]),
                 waits=[("v", b.c("v"))], inc="s")
            sdn = b.c("s")
            b.on("vector", lambda e: TT(e, stage[:], stage2[:], gam[:], ALU.mult),
                 waits=[("s", sdn), ("gpsimddma", consts_g)], inc="v")
            b.on("vector", lambda e: TT(e, stage[:], stage[:], bet[:], ALU.add), inc="v")
            b.on("vector", lambda e: e.tensor_reduce(out=small[:, 14:15], in_=stage[:],
                                                     axis=AXX, op=ALU.max,
                                                     apply_absolute_value=True), inc="v")

            def qsc(e):
                PAD(e)
                e.tensor_scalar_max(small[:, 8:9], small[:, 14:15], 1e-5)
                PAD(e)
                e.reciprocal(out=small[:, 9:10], in_=small[:, 8:9])
                e.reciprocal(out=small[:, 9:10], in_=small[:, 8:9])
                PAD(e)
                e.tensor_scalar_mul(small[:, 10:11], small[:, 9:10], QB)
                e.tensor_scalar_mul(small[:, 11:12], small[:, 8:9], 1.0 / QB)
                PAD(e)
                e.tensor_scalar_mul(small[:, 12:13], small[:, 11:12], -MAGIC)
                PAD(e)
                return e.tensor_copy(out=junk[:, 0:5], in_=small[:, 8:13])
            b.on("vector", qsc, inc="v")
            vdn = b.c("v")
            b.on("scalar", lambda e: e.activation(out=stage2[:], in_=stage[:], func=AF.Identity,
                                                  scale=small[:, 10:11], bias=cst[:, 1:2]),
                 waits=[("v", vdn)], inc="s")
            sdn = b.c("s")
            b.on("vector", lambda e: TSS(e, stage[:], stage2[:], -MAGIC, ALU.add),
                 waits=[("s", sdn)], inc="v")
            vdn = b.c("v")
            b.on("scalar", lambda e: e.activation(out=bstage[:], in_=stage[:], func=AF.Identity,
                                                  scale=small[:, 11:12]),
                 waits=[("v", vdn)], inc="s")
            sdn = b.c("s")
            transpose_to_arena(bstage, arA, NT, tt * 128, (2, 3), [("s", sdn)])
        quant_ok = b.c("v")
        d = dbg_out("qxT", (128, 32 * NT), BF16)
        if d is not None:
            b.dma("sync", d[:, :], arA[:, 0:32 * NT], waits=[("v", quant_ok)])

    b.barrier()

    bank_free = {i: 0 for i in range(8)}
    ev_slot_free = [("v", 0), ("v", 0)]
    st_slot_free = [("v", 0), ("v", 0)]

    def stream_matmul(t_mat, n_out, rank_cols, dest_fn, dest_f32, tern_cc,
                      dma_eng="gpsimd"):
        n_grp = n_out // 4
        nrk = 512 // rank_cols
        for g in range(n_grp):
            r0 = (4 * g * 128) // rank_cols
            ldq = {}

            def emit_load(qt, g=g, r0=r0, ldq=ldq):
                half = qt % 2
                wl = ([("cc", tern_cc)] if qt < 2 else []) + [("t", b.c("t"))]
                last = None
                for ri in range(nrk):
                    src = bass.AP(t_mat,
                                  (r0 + ri) * D * rank_cols + qt * 1024 * rank_cols,
                                  [[rank_cols, 128], [128 * rank_cols, 8],
                                   [1, rank_cols]])
                    dst = tbuf[:].rearrange("p (h c8 w) -> p h c8 w", h=2, c8=8)[
                        :, half, :, ri * rank_cols:(ri + 1) * rank_cols]
                    last = b.dma("sync", dst, src, waits=wl)
                    wl = []
                ldq[qt] = last
            emit_load(0)
            emit_load(1)
            for qt in range(4):
                half = qt % 2
                for pair in range(2):
                    for dc in range(2):
                        oc = 4 * g + pair * 2 + dc
                        co = (oc * 128) % 512
                        for d8 in range(8):
                            first = (qt == 0 and d8 == 0)
                            last = (qt == 3 and d8 == 7)
                            qt_end = (pair == 1 and dc == 1 and d8 == 7)
                            lhs = tbuf[:].rearrange("p (h c8 w) -> p h c8 w",
                                                    h=2, c8=8)[:, half, d8, co:co + 128]
                            for th in range(2):
                                bank = pair * 4 + dc * 2 + th
                                rhs = arA[:, dchoff(qt, d8) + th * 512:
                                          dchoff(qt, d8) + (th + 1) * 512]
                                wl = [("syncdma", ldq[qt])]
                                if first:
                                    wl.append(("sevac", bank_free[bank]))
                                b.on("tensor",
                                     lambda e, bank=bank, lhs=lhs, rhs=rhs,
                                     first=first, last=last: e.matmul(
                                         ps[bank][:, :], lhs, rhs,
                                         start=first, stop=last),
                                     waits=wl,
                                     inc=("t" if (qt_end and th == 1) else None))
                if qt < 2:
                    emit_load(qt + 2)
            tdn = b.c("t")
            for pair in range(2):
                for dc in range(2):
                    oc = 4 * g + pair * 2 + dc
                    for th in range(2):
                        bank = pair * 4 + dc * 2 + th
                        slot = (oc * 2 + th) % 2
                        if dest_f32:
                            b.on("scalar", lambda e, bank=bank, slot=slot: e.activation(
                                    out=stage2[:, slot * 512:(slot + 1) * 512],
                                    in_=ps[bank][:, :], func=AF.Copy),
                                 waits=[("t", tdn), st_slot_free[slot]],
                                 inc="sevac")
                            bank_free[bank] = b.c("sevac")
                            b.dma(dma_eng, dest_fn(oc, th),
                                  stage2[:, slot * 512:(slot + 1) * 512],
                                  waits=[("sevac", b.c("sevac"))])
                            st_slot_free[slot] = (dma_eng + "dma",
                                                  b.c(dma_eng + "dma"))
                        else:
                            b.on("scalar", lambda e, bank=bank, slot=slot: e.activation(
                                    out=evbuf[:, slot * 512:(slot + 1) * 512],
                                    in_=ps[bank][:, :], func=AF.Copy),
                                 waits=[("t", tdn), ev_slot_free[slot]],
                                 inc="sevac")
                            bank_free[bank] = b.c("sevac")
                            b.dma("gpsimd", dest_fn(oc, th),
                                  evbuf[:, slot * 512:(slot + 1) * 512],
                                  waits=[("sevac", b.c("sevac"))])
                            ev_slot_free[slot] = ("gpsimddma", b.c("gpsimddma"))

    def pairbank(slot):
        return 6 + slot

    def dchoff(qt, d8):
        return (qt * 8 + d8) * NT

    if "qkv" in PHASES:
        b.sem("sevac")
        stream_matmul(tq, 32, 512,
                      lambda oc, th: bass.AP(qT_sp, oc * 128 * NT + th * 512,
                                             [[NT, 128], [1, 512]]), True, cc_q)
        stream_matmul(tk, 8, 128,
                      lambda oc, th: bass.AP(kv_in, oc * 128 * 1024 + th * 512,
                                             [[1024, 128], [1, 512]]), True, cc_k)
        stream_matmul(tv, 8, 128,
                      lambda oc, th: bass.AP(kv_in, (1024 + oc * 128) * 1024 + th * 512,
                                             [[1024, 128], [1, 512]]), True, cc_v)
        qkv_g = b.c("gpsimddma")
        b.on("gpsimd", lambda e: e.collective_compute(
                "AllGather", ALU.bypass,
                replica_groups=[[0, 1], [2, 3], [4, 5], [6, 7]],
                ins=[kv_in.ap().opt()], outs=[kv.ap().opt()]),
             waits=[("gpsimddma", qkv_g)], inc="cc")
        kv_ok = b.c("cc")
        d = dbg_out("kv", (2, 2048, 1024), F32)
        if d is not None:
            b.dma("sync", d[:, :, :], kv.ap()[:, :, :], waits=[("cc", kv_ok)])
        d = dbg_out("qT", (D, NT), F32)
        if d is not None:
            b.dma("sync", d[:, :], qT_sp.ap()[:, :], waits=[("gpsimddma", qkv_g)])

    b.barrier()

    if "attn" in PHASES:
        b.sem("sevac")
        def attn_scales(e):
            e.tensor_mul(out=small[:, 28:29], in0=scal[:, 0:1], in1=scal[:, 1:2])
            e.tensor_mul(out=small[:, 29:30], in0=scal[:, 2:3], in1=scal[:, 3:4])
            PAD(e)
            e.tensor_scalar_mul(small[:, 30:31], small[:, 28:29], RSQ)
            PAD(e)
            return e.tensor_copy(out=junk[:, 0:3], in_=small[:, 29:32])
        b.on("vector", attn_scales, inc="v")

        for g in range(8):
            for r in range(2):
                lw = b.dma("sync", stage[:, r * 1024:(r + 1) * 1024],
                           bass.AP(kv, r * 2048 * 1024 + g * 128 * 1024,
                                   [[1024, 128], [1, 1024]]),
                           waits=[("v", b.c("v")), ("cc", kv_ok)])
                b.dma("sync", stage2[0:64, r * 1024:(r + 1) * 1024],
                      bass.AP(kv, r * 2048 * 1024 + (g * 128 + 64) * 1024,
                              [[1024, 64], [1, 1024]]),
                      waits=[("cc", kv_ok)])
                lw = b.dma("sync", stage2[64:128, r * 1024:(r + 1) * 1024],
                           bass.AP(kv, r * 2048 * 1024 + g * 128 * 1024,
                                   [[1024, 64], [1, 1024]]),
                           waits=[("cc", kv_ok)])
            kld = b.c("syncdma")

            def rope_k(e):
                TT(e, stage[:, 2048:4096], stage[:, 0:2048], rcos[:], ALU.mult)
                TT(e, stage2[:, 2048:4096], stage2[:, 0:2048], rsin[:], ALU.mult)
                PAD(e)
                return TT(e, evbuf[:, 0:2048], stage[:, 2048:4096],
                          stage2[:, 2048:4096], ALU.add)
            b.on("vector", rope_k, waits=[("syncdma", kld), ("t", b.c("t"))], inc="v")
            kb_ok = b.c("v")
            b.on("scalar", lambda e: e.activation(out=bstage[:, 0:2048],
                                                  in_=evbuf[:, 0:2048], func=AF.Square),
                 waits=[("v", kb_ok)], inc="s")
            sq_ok = b.c("s")
            for i in range(4):
                b.on("tensor", lambda e, i=i: e.matmul(
                        ps[5][0:1, :], onescol[:, 0:1], bstage[:, i * 512:(i + 1) * 512],
                        start=True, stop=True),
                     waits=[("s", sq_ok), ("v", b.c("v")), ("v", v_init)], inc="t")
                b.on("vector", lambda e, i=i: e.tensor_reduce(
                        out=kvrow[0:1, i:i + 1], in_=ps[5][0:1, :], axis=AXX,
                        op=ALU.max),
                     waits=[("t", b.c("t"))], inc="v")
            def kmax(e):
                PAD(e)
                e.tensor_reduce(out=kvrow[0:1, 4:5], in_=kvrow[0:1, 0:4], axis=AXX,
                                op=ALU.max)
                PAD(e)
                return e.tensor_copy(out=junkr[0:1, 0:1], in_=kvrow[0:1, 4:5])
            b.on("vector", kmax, inc="v")
            b.on("scalar", lambda e: e.activation(out=kvrow[0:1, 5:6],
                                                  in_=kvrow[0:1, 4:5], func=AF.Sqrt),
                 waits=[("v", b.c("v"))], inc="s")
            kn_ok = b.c("s")

            lw = b.dma("sync", stage[:, 0:1024],
                       bass.AP(kv, (1024 + g * 128) * 1024, [[1024, 128], [1, 1024]]),
                       waits=[("v", kb_ok), ("cc", kv_ok)])
            lw = b.dma("sync", stage[:, 1024:2048],
                       bass.AP(kv, 2048 * 1024 + (1024 + g * 128) * 1024,
                               [[1024, 128], [1, 1024]]),
                       waits=[("cc", kv_ok)])
            b.on("vector", lambda e: e.tensor_copy(out=bstage[:, 2048:4096],
                                                   in_=stage[:, 0:2048]),
                 waits=[("syncdma", lw), ("s", sq_ok)], inc="v")
            vcast = b.c("v")
            for grp in range(2):
                bank = 6 + grp
                for j in range(8):
                    b.on("tensor", lambda e, bank=bank, j=j, grp=grp: e.transpose(
                            ps[bank][:, j * 64:(j + 1) * 64].bitcast(BF16),
                            bstage[:, 2048 + grp * 1024 + j * 128:
                                   2048 + grp * 1024 + (j + 1) * 128], ident[:]),
                         waits=[("v", vcast), ("sevac", bank_free.get(("vb", bank), 0))],
                         inc=("t" if j == 7 else None))
                tdn = b.c("t")
                b.on("vector", lambda e, bank=bank, grp=grp: e.tensor_copy(
                        out=evbuf[:, 2048 + grp * 1024:2048 + (grp + 1) * 1024],
                        in_=ps[bank][:, 0:512].bitcast(BF16)),
                     waits=[("t", tdn)], inc="v")
            vb_ok = b.c("v")

            for h in range(4 * g, 4 * g + 4):
                lw = b.dma("sync", stage[:, 0:1024],
                           bass.AP(qT_sp, h * 128 * NT, [[NT, 128], [1, NT]]),
                           waits=[("v", b.c("v")), ("s", b.c("s")), ("t", b.c("t"))])
                b.dma("sync", stage[0:64, 1024:2048],
                      bass.AP(qT_sp, (h * 128 + 64) * NT, [[NT, 64], [1, NT]]))
                lw = b.dma("sync", stage[64:128, 1024:2048],
                           bass.AP(qT_sp, h * 128 * NT, [[NT, 64], [1, NT]]))
                qld = b.c("syncdma")

                def rope_q(e):
                    TT(e, stage[:, 2048:3072], stage[:, 0:1024], rco[:], ALU.mult)
                    TT(e, stage[:, 3072:4096], stage[:, 1024:2048], rso[:], ALU.mult)
                    PAD(e)
                    return TT(e, bstage[:, 0:1024], stage[:, 2048:3072],
                              stage[:, 3072:4096], ALU.add)
                b.on("vector", rope_q, waits=[("syncdma", qld)], inc="v")
                qb_ok = b.c("v")
                b.on("scalar", lambda e: e.activation(out=bstage[:, 1024:2048],
                                                      in_=bstage[:, 0:1024],
                                                      func=AF.Square),
                     waits=[("v", qb_ok)], inc="s")
                qsq_ok = b.c("s")
                for i in range(2):
                    b.on("tensor", lambda e, i=i: e.matmul(
                            ps[5][0:1, :], onescol[:, 0:1],
                            bstage[:, 1024 + i * 512:1024 + (i + 1) * 512],
                            start=True, stop=True),
                         waits=[("s", qsq_ok), ("v", b.c("v"))], inc="t")
                    b.on("vector", lambda e, i=i: e.tensor_reduce(
                            out=kvrow[0:1, 8 + i:9 + i], in_=ps[5][0:1, :], axis=AXX,
                            op=ALU.max),
                         waits=[("t", b.c("t"))], inc="v")

                def ch_calc(e):
                    PAD(e)
                    e.tensor_reduce(out=kvrow[0:1, 10:11], in_=kvrow[0:1, 8:10],
                                    axis=AXX, op=ALU.max)
                    PAD(e)
                    return e.tensor_copy(out=junkr[0:1, 0:1], in_=kvrow[0:1, 10:11])
                b.on("vector", ch_calc, inc="v")
                b.on("scalar", lambda e: e.activation(out=kvrow[0:1, 11:12],
                                                      in_=kvrow[0:1, 10:11],
                                                      func=AF.Sqrt),
                     waits=[("v", b.c("v"))], inc="s")

                def negc(e):
                    e.tensor_mul(out=kvrow[0:1, 12:13], in0=kvrow[0:1, 11:12],
                                 in1=kvrow[0:1, 5:6])
                    PAD(e)
                    e.scalar_tensor_tensor(out=kvrow[0:1, 14:15],
                                           in0=kvrow[0:1, 12:13], scalar=-1.0,
                                           in1=small[0:1, 30:31],
                                           op0=ALU.mult, op1=ALU.mult)
                    PAD(e)
                    e.tensor_copy(out=kvrow.bitcast(BF16)[0:1, 26:27],
                                  in_=kvrow[0:1, 14:15])
                    PAD(e)
                    return e.tensor_copy(out=junkr[0:1, 0:8], in_=kvrow[0:1, 8:16])
                b.on("vector", negc, waits=[("s", b.c("s")), ("s", kn_ok)], inc="v")
                b.on("tensor", lambda e: e.matmul(
                        ps[5][:, 0:1], onesrow[0:1, :], kvrow.bitcast(BF16)[0:1, 26:27],
                        start=True, stop=True),
                     waits=[("v", b.c("v"))], inc="t")
                def negc_col(e):
                    e.tensor_copy(out=small[:, 31:32], in_=ps[5][:, 0:1])
                    return e.tensor_copy(out=junk[:, 0:1], in_=small[:, 31:32])
                b.on("vector", negc_col, waits=[("t", b.c("t"))], inc="v")
                negc_ok = b.c("v")

                for kc in range(16):
                    for qh in range(2):
                        bank = (kc * 2 + qh) % 4
                        wl = [("v", qb_ok), ("v", kb_ok),
                              ("sevac", bank_free.get(("sc", bank), 0))]
                        b.on("tensor", lambda e, bank=bank, kc=kc, qh=qh: e.matmul(
                                ps[bank][:, :], evbuf[:, kc * 128:(kc + 1) * 128],
                                bstage[:, qh * 512:(qh + 1) * 512],
                                start=True, stop=True),
                             waits=wl, inc="t")
                        tdn = b.c("t")
                        b.on("scalar", lambda e, bank=bank, kc=kc, qh=qh: e.activation(
                                out=arB[:, qh * 8192 + kc * 512:qh * 8192 + (kc + 1) * 512],
                                in_=ps[bank][:, :], func=AF.Exp,
                                scale=small[:, 30:31], bias=small[:, 31:32]),
                             waits=[("t", tdn), ("v", negc_ok)], inc="sevac")
                        bank_free[("sc", bank)] = b.c("sevac")
                p_ok = b.c("sevac")
                for qh in range(2):
                    for kc in range(16):
                        wl = [("sevac", p_ok), ("v", negc_ok)]
                        if kc == 0:
                            wl.append(("v", bank_free.get(("sums", qh), 0)))
                        b.on("tensor", lambda e, qh=qh, kc=kc: e.matmul(
                                ps[4 + qh][0:1, :], onescol[:, 0:1],
                                arB[:, qh * 8192 + kc * 512:qh * 8192 + (kc + 1) * 512],
                                start=(kc == 0), stop=(kc == 15)),
                             waits=wl,
                             inc=("t" if kc == 15 else None))
                    tdn = b.c("t")
                    def sums_ev(e, qh=qh):
                        e.tensor_copy(out=srow[0:1, qh * 512:(qh + 1) * 512],
                                      in_=ps[4 + qh][0:1, :])
                        return e.tensor_copy(out=junkr[0:1, 0:8],
                                             in_=srow[0:1, qh * 512 + 504:qh * 512 + 512])
                    b.on("vector", sums_ev,
                         waits=[("t", tdn),
                                ("gpsimddma", bank_free.get(("srow",), 0))],
                         inc="v")
                    bank_free[("sums", qh)] = b.c("v")
                vdn = b.c("v")
                b.dma("gpsimd", bass.AP(sums_sp, h * 1024, [[1, 1024]]),
                      srow[0:1, 0:1024], waits=[("v", vdn)])
                bank_free[("srow",)] = b.c("gpsimddma")
                for qh in range(2):
                    for kc in range(16):
                        wl = [("sevac", p_ok), ("v", vb_ok)]
                        if kc == 0:
                            wl.append(("sevac", bank_free.get(("ov", qh), 0)))
                        b.on("tensor", lambda e, qh=qh, kc=kc: e.matmul(
                                ps[6 + qh][:, :], evbuf[:, 2048 + kc * 128:
                                                        2048 + (kc + 1) * 128],
                                arB[:, qh * 8192 + kc * 512:qh * 8192 + (kc + 1) * 512],
                                start=(kc == 0), stop=(kc == 15)),
                             waits=wl, inc=("t" if kc == 15 else None))
                    tdn = b.c("t")
                    b.on("scalar", lambda e, qh=qh, h=h: e.activation(
                            out=arA[:, h * NT + qh * 512:h * NT + (qh + 1) * 512],
                            in_=ps[6 + qh][:, :], func=AF.Copy),
                         waits=[("t", tdn)], inc="sevac")
                    bank_free[("ov", qh)] = b.c("sevac")
                bank_free[("vb", 6)] = b.c("sevac")
                bank_free[("vb", 7)] = b.c("sevac")
        oT_ok = b.c("sevac")
        d = dbg_out("oT", (128, 32 * NT), BF16)
        if d is not None:
            b.dma("sync", d[:, :], arA[:, 0:32 * NT], waits=[("sevac", oT_ok)])

        d = dbg_out("sums", (32 * NT,), F32)
        if d is not None:
            b.dma("sync", d[:], bass.AP(sums_sp, 0, [[1, 32 * NT]]),
                  waits=[("gpsimddma", b.c("gpsimddma"))])
        lw = b.dma("sync", sumscol[:, :], bass.AP(sums_sp, 0, [[1, 128], [128, 256]]),
                   waits=[("gpsimddma", b.c("gpsimddma"))],
                   allow_slow_non_contiguous=True)
        def rsum_fin(e):
            e.reciprocal(out=stage[:, 0:256], in_=sumscol[:, :])
            e.reciprocal(out=stage[:, 0:256], in_=sumscol[:, :])
            PAD(e)
            return e.tensor_copy(out=sumscol[:, :], in_=stage[:, 0:256])
        b.on("vector", rsum_fin, waits=[("syncdma", lw)], inc="v")
        rs_ok = b.c("v")
        d = dbg_out("rsums", (128, 256), F32)
        if d is not None:
            b.dma("sync", d[:, :], sumscol[:, :], waits=[("v", rs_ok)])

        for tt in range(8):
            for grp in range(4):
                bank = grp % 2
                for j in range(8):
                    hh = grp * 8 + j
                    wl = [("sevac", oT_ok), ("v", v_init),
                          ("s", bank_free.get(("ot", bank), 0))]
                    b.on("tensor", lambda e, bank=bank, j=j, hh=hh, tt=tt: e.transpose(
                            ps[bank][:, j * 64:(j + 1) * 64].bitcast(BF16),
                            arA[:, hh * NT + tt * 128:hh * NT + tt * 128 + 128],
                            ident[:]),
                         waits=wl, inc=("t" if j == 7 else None))
                tdn = b.c("t")
                for j in range(8):
                    hh = grp * 8 + j
                    b.on("scalar", lambda e, bank=bank, j=j, hh=hh, tt=tt: e.activation(
                            out=bstage[:, hh * 128:(hh + 1) * 128],
                            in_=ps[bank][:, j * 64:(j + 1) * 64].bitcast(BF16),
                            func=AF.Copy, scale=sumscol[:, hh * 8 + tt:hh * 8 + tt + 1]),
                         waits=[("t", tdn), ("v", rs_ok)],
                         inc=("s" if j == 7 else None))
                bank_free[("ot", bank)] = b.c("s")
            s_ok = b.c("s")
            b.on("vector", lambda e: e.tensor_copy(out=stage[:], in_=bstage[:]),
                 waits=[("s", s_ok)], inc="v")
            b.on("vector", lambda e: e.tensor_reduce(out=small[:, 14:15], in_=stage[:],
                                                     axis=AXX, op=ALU.max,
                                                     apply_absolute_value=True), inc="v")

            def qsc2(e):
                PAD(e)
                e.tensor_scalar_max(small[:, 8:9], small[:, 14:15], 1e-5)
                PAD(e)
                e.reciprocal(out=small[:, 9:10], in_=small[:, 8:9])
                e.reciprocal(out=small[:, 9:10], in_=small[:, 8:9])
                PAD(e)
                e.tensor_scalar_mul(small[:, 10:11], small[:, 9:10], QB)
                e.tensor_scalar_mul(small[:, 11:12], small[:, 8:9], 1.0 / QB)
                PAD(e)
                e.tensor_scalar_mul(small[:, 12:13], small[:, 11:12], -MAGIC)
                PAD(e)
                return e.tensor_copy(out=junk[:, 0:5], in_=small[:, 8:13])
            b.on("vector", qsc2, inc="v")
            vdn = b.c("v")
            b.on("scalar", lambda e: e.activation(out=stage2[:], in_=stage[:],
                                                  func=AF.Identity, scale=small[:, 10:11],
                                                  bias=cst[:, 1:2]),
                 waits=[("v", vdn)], inc="s")
            sdn = b.c("s")
            b.on("vector", lambda e: TSS(e, stage[:], stage2[:], -MAGIC, ALU.add),
                 waits=[("s", sdn)], inc="v")
            vdn = b.c("v")
            b.on("scalar", lambda e: e.activation(out=bstage[:], in_=stage[:],
                                                  func=AF.Identity, scale=small[:, 11:12]),
                 waits=[("v", vdn)], inc="s")
            sdn = b.c("s")
            lw = b.dma("sync", att_sp.ap()[tt * 128:(tt + 1) * 128, :], bstage[:],
                       waits=[("s", sdn)])
            b.on("scalar", lambda e: e.nop(), waits=[("syncdma", lw)], inc="s")
            bank_free[("ot", 0)] = b.c("s")
            bank_free[("ot", 1)] = b.c("s")

        d = dbg_out("qO", (NT, D), BF16)
        if d is not None:
            b.dma("sync", d[:, :], att_sp.ap()[:, :], waits=[("s", b.c("s"))])

        for tt in range(8):
            lw = b.dma("sync", bstage[:], att_sp.ap()[tt * 128:(tt + 1) * 128, :],
                       waits=[("v", b.c("v"))])
            transpose_to_arena(bstage, arA, NT, tt * 128, (2, 3),
                               [("syncdma", lw)])
        qoT_ok = b.c("v")

        for gq in range(8):
            ldq = {}

            def emit_load(qt, gq=gq, ldq=ldq):
                wl = (([("v", qoT_ok), ("cc", cc_o)] if qt < 2 else [])
                      + [("t", b.c("t"))])
                src = bass.AP(to, gq * D * 512 + qt * 1024 * 512,
                              [[512, 128], [128 * 512, 8], [1, 512]])
                dst = tbuf[:].rearrange("p (h c8 w) -> p h c8 w", h=2, c8=8)[
                    :, qt % 2, :, :]
                ldq[qt] = b.dma("sync", dst, src, waits=wl)
            emit_load(0)
            emit_load(1)
            for qt in range(4):
                half = qt % 2
                for pair in range(2):
                    for dc in range(2):
                        oc = 4 * gq + pair * 2 + dc
                        co = (oc * 128) % 512
                        for d8 in range(8):
                            first = (qt == 0 and d8 == 0)
                            last = (qt == 3 and d8 == 7)
                            qt_end = (pair == 1 and dc == 1 and d8 == 7)
                            lhs = tbuf[:].rearrange("p (h c8 w) -> p h c8 w",
                                                    h=2, c8=8)[:, half, d8, co:co + 128]
                            for th in range(2):
                                bank = pair * 4 + dc * 2 + th
                                rhs = arA[:, (qt * 8 + d8) * NT + th * 512:
                                          (qt * 8 + d8) * NT + (th + 1) * 512]
                                wl = [("syncdma", ldq[qt])]
                                if first:
                                    wl.append(("sevac", bank_free[bank]))
                                b.on("tensor",
                                     lambda e, bank=bank, lhs=lhs, rhs=rhs,
                                     first=first, last=last: e.matmul(
                                         ps[bank][:, :], lhs, rhs,
                                         start=first, stop=last),
                                     waits=wl,
                                     inc=("t" if (qt_end and th == 1) else None))
                if qt < 2:
                    emit_load(qt + 2)
            tdn = b.c("t")
            for pair in range(2):
                for dc in range(2):
                    oc = 4 * gq + pair * 2 + dc
                    for th in range(2):
                        bank = pair * 4 + dc * 2 + th
                        slot = th
                        b.on("scalar", lambda e, bank=bank, slot=slot: e.activation(
                                out=evbuf[:, slot * 512:(slot + 1) * 512],
                                in_=ps[bank][:, :], func=AF.Copy),
                             waits=[("t", tdn), ev_slot_free[slot]],
                             inc="sevac")
                        sdn = b.c("sevac")
                        for j in range(4):
                            b.on("tensor", lambda e, j=j, bank=bank, slot=slot: e.transpose(
                                    ps[bank][:, j * 64:(j + 1) * 64].bitcast(BF16),
                                    evbuf[:, slot * 512 + j * 128:
                                          slot * 512 + (j + 1) * 128], ident[:]),
                                 waits=[("sevac", sdn)],
                                 inc=("t" if j == 3 else None))
                        tdn2 = b.c("t")
                        b.on("vector", lambda e, bank=bank, slot=slot: e.tensor_copy(
                                out=evbuf[:, 2048 + slot * 512:2048 + (slot + 1) * 512],
                                in_=ps[bank][:, 0:256].bitcast(BF16)),
                             waits=[("t", tdn2)], inc="sevac")
                        vdn = b.c("sevac")
                        bank_free[bank] = vdn
                        for j in range(4):
                            b.dma("gpsimd",
                                  bass.AP(att_sp, (th * 512 + j * 128) * D + oc * 128,
                                          [[D, 128], [1, 128]]),
                                  evbuf[:, 2048 + slot * 512 + j * 128:
                                        2048 + slot * 512 + (j + 1) * 128],
                                  waits=[("sevac", vdn)])
                        ev_slot_free[slot] = ("gpsimddma", b.c("gpsimddma"))
        att_g = b.c("gpsimddma")
        for tt in range(8):
            lw = b.dma("sync", bstage[:], att_sp.ap()[tt * 128:(tt + 1) * 128, :],
                       waits=[("gpsimddma", att_g), ("v", b.c("v"))])
            lw2 = b.dma("sync", stage2[:], x_own.ap()[tt * 128:(tt + 1) * 128, :])
            b.on("vector", lambda e: e.scalar_tensor_tensor(
                    out=stage[:], in0=bstage[:], scalar=small[:, 29:30], in1=stage2[:],
                    op0=ALU.mult, op1=ALU.add),
                 waits=[("syncdma", lw2)], inc="v")
            vdn = b.c("v")
            lw = b.dma("sync", res_a.ap()[tt * 128:(tt + 1) * 128, :], stage[:],
                       waits=[("v", vdn)])
            b.on("vector", lambda e: e.nop(), waits=[("syncdma", lw)], inc="v")
        d = dbg_out("x1", (NT, D), F32)
        if d is not None:
            b.dma("sync", d[:, :], res_a.ap()[:, :],
                  waits=[("syncdma", b.c("syncdma"))])

    b.barrier()

    def ffn_phase(src_res, dst_res, tag):
        b.sem("sevac")
        for tt in range(8):
            lw = b.dma("sync", stage[:], src_res.ap()[tt * 128:(tt + 1) * 128, :],
                       waits=[("v", b.c("v")), ("s", b.c("s"))])
            b.on("vector", lambda e: ln_stats(e, stage),
                 waits=[("syncdma", lw)], inc="v")
            ln_coeffs(2, 3)
            b.on("scalar", lambda e: e.activation(out=bstage[:], in_=stage[:],
                                                  func=AF.Identity, scale=small[:, 2:3],
                                                  bias=small[:, 3:4]),
                 waits=[("v", b.c("v"))], inc="s")
            sdn = b.c("s")
            transpose_to_arena(bstage, arA, NT, tt * 128, (2, 3), [("s", sdn)])
        xnT_ok = b.c("v")
        lw = b.dma("sync", bass.AP(fg_a, 0, [[NT, 128], [128 * NT, 32], [1, NT]]),
                   arA[:, 0:32768], waits=[("v", xnT_ok)])
        ga = []
        for c in range(32):
            w = b.on("gpsimd", lambda e, c=c: e.indirect_dma_start(
                    out=arA[:, c * NT:(c + 1) * NT], out_offset=None,
                    in_=fg_a.ap()[0:128, :],
                    in_offset=IndirectOffsetOnAxis(idxs[:, c:c + 1], 0)),
                 waits=[("syncdma", lw)], inc="gpsimddma", inc_n=16)
        xgT_ok = b.c("gpsimddma")
        for h in range(16):
            lw = b.dma("sync", wbuf[:],
                       bass.AP(up_h, h * 256 * 1024,
                               [[1024, 128], [128 * 1024, 2], [1, 1024]]),
                       waits=[("v", b.c("v")), ("t", b.c("t"))])
            b.on("vector", lambda e: e.tensor_copy(out=tbuf[:, 0:2048], in_=wbuf[:]),
                 waits=[("syncdma", lw)], inc="v")
            upb_ok = b.c("v")
            lw = b.dma("sync", wbuf[:],
                       bass.AP(dn_h, h * 1024 * 256,
                               [[256, 128], [128 * 256, 8], [1, 256]]),
                       waits=[("v", upb_ok)])
            b.on("vector", lambda e: e.tensor_copy(out=tbuf[:, 2048:4096], in_=wbuf[:]),
                 waits=[("syncdma", lw)], inc="v")
            dnb_ok = b.c("v")
            for mc in range(8):
                for th in range(2):
                    bank = (mc % 2) * 2 + th
                    for dc in range(2):
                        wl = [("v", upb_ok),
                              ("gpsimddma", xgT_ok)]
                        if dc == 0:
                            wl.append(("sevac", bank_free.get((tag, bank), 0)))
                        b.on("tensor", lambda e, bank=bank, mc=mc, th=th, dc=dc, h=h:
                             e.matmul(ps[bank][:, :],
                                      tbuf[:, dc * 1024 + mc * 128:
                                           dc * 1024 + (mc + 1) * 128],
                                      arA[:, (2 * h + dc) * NT + th * 512:
                                          (2 * h + dc) * NT + (th + 1) * 512],
                                      start=(dc == 0), stop=(dc == 1)),
                             waits=wl, inc=("t" if dc == 1 else None))
                    tdn = b.c("t")
                    b.on("scalar", lambda e, bank=bank, mc=mc, th=th: e.activation(
                            out=arB[:, mc * 1024 + th * 512:mc * 1024 + (th + 1) * 512],
                            in_=ps[bank][:, :], func=AF.Gelu),
                         waits=[("t", tdn)], inc="sevac")
                    bank_free[(tag, bank)] = b.c("sevac")
            g_ok = b.c("sevac")
            for oc in range(2):
                slot = oc
                for th in range(2):
                    bank = 4 + oc * 2 + th
                    for mc in range(8):
                        wl = [("sevac", g_ok), ("v", dnb_ok)]
                        if mc == 0:
                            wl.append(("sevac", bank_free.get((tag, bank), 0)))
                        b.on("tensor", lambda e, bank=bank, oc=oc, th=th, mc=mc:
                             e.matmul(ps[bank][:, :],
                                      tbuf[:, 2048 + mc * 256 + oc * 128:
                                           2048 + mc * 256 + (oc + 1) * 128],
                                      arB[:, mc * 1024 + th * 512:
                                          mc * 1024 + (th + 1) * 512],
                                      start=(mc == 0), stop=(mc == 7)),
                             waits=wl, inc=("t" if mc == 7 else None))
                    tdn = b.c("t")
                    b.on("scalar", lambda e, bank=bank, slot=slot, th=th: e.activation(
                            out=evbuf[:, slot * 1024 + th * 512:
                                      slot * 1024 + (th + 1) * 512],
                            in_=ps[bank][:, :], func=AF.Copy),
                         waits=[("t", tdn), ev_slot_free[slot]],
                         inc="sevac")
                    bank_free[(tag, bank)] = b.c("sevac")
                sdn = b.c("sevac")
                b.on("gpsimd", lambda e, h=h, oc=oc, slot=slot:
                     e.indirect_dma_start(
                        out=fg_b.ap()[0:128, :],
                        out_offset=IndirectOffsetOnAxis(
                            idxs[:, 2 * h + oc:2 * h + oc + 1], 0),
                        in_=evbuf[:, slot * 1024:(slot + 1) * 1024],
                        in_offset=None),
                     waits=[("sevac", sdn)], inc="gpsimddma", inc_n=16)
                ev_slot_free[slot] = ("gpsimddma", b.c("gpsimddma"))
        sc_ok = b.c("gpsimddma")
        lw = b.dma("sync", arA[:, 0:32768],
                   bass.AP(fg_b, 0, [[NT, 128], [128 * NT, 32], [1, NT]]),
                   waits=[("gpsimddma", sc_ok), ("sevac", b.c("sevac")),
                          ("t", b.c("t"))])
        ffnT_ok = lw
        for tt in range(8):
            for grp in range(4):
                bank = grp % 2
                for j in range(8):
                    cc = grp * 8 + j
                    wl = [("syncdma", ffnT_ok), ("v", v_init),
                          ("v", bank_free.get((tag, "fb", bank), 0))]
                    b.on("tensor", lambda e, bank=bank, j=j, cc=cc, tt=tt: e.transpose(
                            ps[bank][:, j * 64:(j + 1) * 64].bitcast(BF16),
                            arA[:, cc * NT + tt * 128:cc * NT + tt * 128 + 128],
                            ident[:]),
                         waits=wl, inc=("t" if j == 7 else None))
                tdn = b.c("t")
                b.on("vector", lambda e, bank=bank, grp=grp: e.tensor_copy(
                        out=bstage[:, grp * 1024:(grp + 1) * 1024],
                        in_=ps[bank][:, 0:512].bitcast(BF16)),
                     waits=[("t", tdn)], inc="v")
                bank_free[(tag, "fb", bank)] = b.c("v")
            lw2 = b.dma("sync", stage2[:], src_res.ap()[tt * 128:(tt + 1) * 128, :],
                        waits=[("v", b.c("v"))])
            b.on("vector", lambda e: TT(e, stage[:], stage2[:], bstage[:], ALU.add),
                 waits=[("syncdma", lw2), ("v", b.c("v"))], inc="v")
            vdn = b.c("v")
            lw = b.dma("sync", dst_res.ap()[tt * 128:(tt + 1) * 128, :], stage[:],
                       waits=[("v", vdn)])
            b.on("vector", lambda e: e.nop(), waits=[("syncdma", lw)], inc="v")

    if "ffn1" in PHASES:
        ffn_phase(res_a, res_b, "f1")
        d = dbg_out("x2", (NT, D), F32)
        if d is not None:
            b.dma("sync", d[:, :], res_b.ap()[:, :],
                  waits=[("syncdma", b.c("syncdma"))])

    b.barrier()

    if "attnb" in PHASES:
        b.sem("sevac")
        for tt in range(8):
            lw = b.dma("sync", stage[:], res_b.ap()[tt * 128:(tt + 1) * 128, :],
                       waits=[("v", b.c("v")), ("s", b.c("s"))])
            b.on("vector", lambda e: ln_stats(e, stage),
                 waits=[("syncdma", lw)], inc="v")
            ln_coeffs(2, 3)
            b.on("scalar", lambda e: e.activation(out=bstage[:], in_=stage[:],
                                                  func=AF.Identity, scale=small[:, 2:3],
                                                  bias=small[:, 3:4]),
                 waits=[("v", b.c("v"))], inc="s")
            sdn = b.c("s")
            transpose_to_arena(bstage, arA, NT, tt * 128, (2, 3), [("s", sdn)])
        xnT_ok = b.c("v")
        def rs_cols(e):
            e.memset(small[:, 26:27], 2.0 / 4096.0)
            e.memset(small[:, 24:25], 2.0 / 4096.0)
            e.memset(small[:, 25:26], 0.0)
            PAD(e)
            e.memset(small[0:1, 24:25], 1.0 / 4096.0)
            e.memset(small[0:1, 25:26], 1.0 / 4096.0)
            PAD(e)
            return e.tensor_copy(out=junk[:, 0:3], in_=small[:, 24:27])
        b.on("vector", rs_cols, inc="v")
        rs_ok2 = b.c("v")

        for f in range(NFC):
            lwc = b.dma("sync", tbuf[:].rearrange("p (a c w) -> p a c w", a=2, c=8)[
                            :, 0, :, :].rearrange("p c w -> p (c w)") if False else
                        tbuf[:, 0:4096],
                        bass.AP(cos_ch, f * D * 128, [[128, 128], [128 * 128, 32],
                                                      [1, 128]]),
                        waits=[("t", b.c("t"))])
            lws = b.dma("sync", tbuf[:, 4096:8192],
                        bass.AP(sin_ch, f * D * 128, [[128, 128], [128 * 128, 32],
                                                      [1, 128]]))
            for th in range(2):
                for ri in range(2):
                    bank = (f % 2) * 4 + th * 2 + ri
                    for dc in range(32):
                        wl = [("syncdma", lws), ("v", xnT_ok)]
                        if dc == 0:
                            wl.append(("sevac", bank_free.get(("fw", bank), 0)))
                        b.on("tensor", lambda e, bank=bank, dc=dc, ri=ri, th=th:
                             e.matmul(ps[bank][:, :],
                                      tbuf[:, ri * 4096 + dc * 128:
                                           ri * 4096 + (dc + 1) * 128],
                                      arA[:, dc * NT + th * 512:
                                          dc * NT + (th + 1) * 512],
                                      start=(dc == 0), stop=(dc == 31)),
                             waits=wl, inc=("t" if dc == 31 else None))
            tdn = b.c("t")
            for th in range(2):
                b.on("scalar", lambda e, f=f, th=th: e.activation(
                        out=stage[:, th * 1024:th * 1024 + 512],
                        in_=ps[(f % 2) * 4 + th * 2][:, :], func=AF.Copy),
                     waits=[("t", tdn)], inc="s")
                b.on("scalar", lambda e, f=f, th=th: e.activation(
                        out=stage[:, th * 1024 + 512:th * 1024 + 1024],
                        in_=ps[(f % 2) * 4 + th * 2 + 1][:, :], func=AF.Copy),
                     inc="s")
            sdn = b.c("s")
            b.on("vector", lambda e: e.nop(), waits=[("s", sdn)], inc="sevac")
            bank_free[("fw", (f % 2) * 4 + 0)] = b.c("sevac")
            bank_free[("fw", (f % 2) * 4 + 1)] = b.c("sevac")
            bank_free[("fw", (f % 2) * 4 + 2)] = b.c("sevac")
            bank_free[("fw", (f % 2) * 4 + 3)] = b.c("sevac")
            def pchunk(e, f=f):
                TT(e, stage2[:, 0:512], stage[:, 0:512], stage[:, 0:512], ALU.mult)
                TT(e, stage2[:, 512:1024], stage[:, 512:1024], stage[:, 512:1024],
                   ALU.mult)
                TT(e, stage2[:, 1024:1536], stage[:, 1024:1536], stage[:, 1024:1536],
                   ALU.mult)
                TT(e, stage2[:, 1536:2048], stage[:, 1536:2048], stage[:, 1536:2048],
                   ALU.mult)
                PAD(e)
                TT(e, stage2[:, 2048:2560], stage2[:, 0:512], stage2[:, 512:1024],
                   ALU.add)
                return TT(e, stage2[:, 2560:3072], stage2[:, 1024:1536],
                          stage2[:, 1536:2048], ALU.add)
            b.on("vector", pchunk, waits=[("s", sdn)], inc="v")
            rscol = 24 if f == 0 else (25 if f == 16 else 26)

            def pscale(e, f=f, rscol=rscol):
                e.tensor_scalar(out=arB[:, f * NT:f * NT + 512],
                                in0=stage2[:, 2048:2560],
                                scalar1=small[:, rscol:rscol + 1],
                                scalar2=None, op0=ALU.mult)
                return e.tensor_scalar(out=arB[:, f * NT + 512:(f + 1) * NT],
                                       in0=stage2[:, 2560:3072],
                                       scalar1=small[:, rscol:rscol + 1],
                                       scalar2=None, op0=ALU.mult)
            b.on("vector", pscale, waits=[("v", rs_ok2)], inc="v")
            def xspill(e, f=f):
                e.tensor_copy(out=evbuf[:, 0:512], in_=stage[:, 0:512])
                e.tensor_copy(out=evbuf[:, 512:1024], in_=stage[:, 1024:1536])
                e.tensor_copy(out=evbuf[:, 1024:1536], in_=stage[:, 512:1024])
                return e.tensor_copy(out=evbuf[:, 1536:2048], in_=stage[:, 1536:2048])
            b.on("vector", xspill, waits=[("gpsimddma", b.c("gpsimddma"))], inc="v")
            vdn = b.c("v")
            b.dma("gpsimd", bass.AP(xr_sp, f * 128 * NT, [[NT, 128], [1, NT]]),
                  evbuf[:, 0:1024], waits=[("v", vdn)])
            b.dma("gpsimd", bass.AP(xi_sp, f * 128 * NT, [[NT, 128], [1, NT]]),
                  evbuf[:, 1024:2048], waits=[("v", vdn)])
            b.on("scalar", lambda e: e.nop(), waits=[("v", vdn)], inc="s")
        p_ok = b.c("v")

        for lb in range(5):
            nlag = 512 if lb < 4 else 128
            for f in range(NFC):
                ncb = nlag // 128
                lw = None
                for cb in range(ncb):
                    cidx = (lb * 512 + cb * 128) // 128
                    lw = b.dma("scalar", evbuf[:, cb * 128:(cb + 1) * 128],
                               bass.AP(cos_ch, cidx * D * 128 + f * 128 * 128,
                                       [[128, 128], [1, 128]]),
                               waits=[("t", b.c("t"))])
                for tt in range(8):
                    wl = [("scalardma", lw), ("v", p_ok)]
                    if f == 0:
                        wl.append(("sevac", bank_free.get(("co", tt), 0)))
                    b.on("tensor", lambda e, tt=tt, f=f, nlag=nlag: e.matmul(
                            ps[tt][:, 0:nlag],
                            arB[:, f * NT + tt * 128:f * NT + (tt + 1) * 128],
                            evbuf[:, 0:nlag], start=(f == 0), stop=(f == NFC - 1)),
                         waits=wl, inc=("t" if tt == 7 else None))
            tdn = b.c("t")
            for tt in range(8):
                b.on("vector", lambda e, tt=tt, lb=lb, nlag=nlag: e.tensor_copy(
                        out=stage[:, 0:nlag] if False else stage2[:, 0:nlag],
                        in_=ps[tt][:, 0:nlag]),
                     waits=[("t", tdn), ("gpsimddma", b.c("gpsimddma"))], inc="v")
                vdn = b.c("v")
                b.dma("gpsimd", bass.AP(corr_sp, tt * 128 * NF + lb * 512,
                                        [[NF, 128], [1, nlag]]),
                      stage2[:, 0:nlag], waits=[("v", vdn)])
                b.on("vector", lambda e: e.nop(),
                     waits=[("gpsimddma", b.c("gpsimddma"))], inc="sevac")
                bank_free[("co", tt)] = b.c("sevac")
        corr_g = b.c("gpsimddma")

        for tt in range(8):
            lw = b.dma("sync", stage[:, 0:2176],
                       bass.AP(corr_sp, tt * 128 * NF, [[NF, 128], [1, NF]]),
                       waits=[("gpsimddma", corr_g), ("v", b.c("v"))])
            b.on("vector", lambda e: e.memset(stage[:, 2049:2176], NEGINF),
                 waits=[("syncdma", lw)], inc="v")
            b.on("vector", lambda e: e.max(out=topk[:, 0:8], in_=stage[:, 0:2176]),
                 inc="v")
            b.on("vector", lambda e: e.max_index(out=ixu[:, 0:8], in_max=topk[:, 0:8],
                                                 in_values=stage[:, 0:2176]), inc="v")

            def expand(e):
                e.tensor_copy(out=topk[:, 8:16], in_=ixu[:, 0:8])
                PAD(e)
                TSS(e, topk[:, 16:24], topk[:, 8:16], 0.0, ALU.is_equal)
                TSS(e, topk[:, 24:32], topk[:, 8:16], 2048.0, ALU.is_equal)
                PAD(e)
                TT(e, topk[:, 32:40], topk[:, 16:24], topk[:, 24:32], ALU.add)
                PAD(e)
                e.tensor_scalar(out=topk[:, 40:48], in0=topk[:, 32:40], scalar1=-1.0,
                                scalar2=2.0, op0=ALU.mult, op1=ALU.add)
                e.memset(topk[:, 48:49], 0.0)
                PAD(e)
                for k in range(1, 8):
                    TT(e, topk[:, 48 + k:49 + k], topk[:, 47 + k:48 + k],
                       topk[:, 39 + k:40 + k], ALU.add)
                    PAD(e)
                e.tensor_scalar(out=topk[:, 56:64], in0=topk[:, 48:56], scalar1=-1.0,
                                scalar2=8.0, op0=ALU.mult, op1=ALU.add)
                PAD(e)
                TT(e, topk[:, 16:24], topk[:, 56:64], topk[:, 40:48], ALU.min)
                PAD(e)
                e.tensor_scalar_max(topk[:, 24:32], topk[:, 16:24], 0.0)
                e.tensor_scalar_mul(small[:, 21:22], topk[:, 0:1], -1.0)
                PAD(e)
                return e.tensor_copy(out=junk[:, 0:1], in_=small[:, 21:22])
            b.on("vector", expand, inc="v")
            b.on("scalar", lambda e: e.activation(out=topk[:, 32:40], in_=topk[:, 0:8],
                                                  func=AF.Exp, bias=small[:, 21:22]),
                 waits=[("v", b.c("v"))], inc="s")

            def wfold(e):
                TT(e, topk[:, 8:16], topk[:, 24:32], topk[:, 32:40], ALU.mult)
                PAD(e)
                e.reduce_sum(out=small[:, 22:23], in_=topk[:, 8:16], axis=AXX)
                PAD(e)
                e.reciprocal(out=small[:, 20:21], in_=small[:, 22:23])
                e.reciprocal(out=small[:, 20:21], in_=small[:, 22:23])
                PAD(e)
                e.tensor_scalar(out=topkb[:, 0:8], in0=topk[:, 8:16],
                                scalar1=small[:, 20:21], scalar2=None,
                                op0=ALU.mult)
                e.tensor_copy(out=junku[:, 0:8], in_=ixu[:, 0:8])
                PAD(e)
                return e.tensor_copy(out=junkb[:, 0:8], in_=topkb[:, 0:8])
            b.on("vector", wfold, waits=[("s", b.c("s"))], inc="v")
            wf_ok = b.c("v")
            for j in range(8):
                gl = b.on("gpsimd", lambda e, j=j: e.indirect_dma_start(
                        out=(evbuf[:, 0:2176] if j % 2 == 0 else tbuf[:, 0:2176]),
                        out_offset=None, in_=cosrow_h.ap()[0:128, :],
                        in_offset=IndirectOffsetOnAxis(
                            ixu.bitcast(I32)[:, j:j + 1], 0)),
                     waits=[("v", wf_ok), ("v", b.c("v"))],
                     inc="gpsimddma", inc_n=16)
                gdn = b.c("gpsimddma")
                if j == 0:
                    b.on("vector", lambda e, j=j: e.tensor_scalar(
                            out=bstage[:, 0:2176], in0=evbuf[:, 0:2176],
                            scalar1=topkb[:, j:j + 1], scalar2=None, op0=ALU.mult),
                         waits=[("gpsimddma", gdn)], inc="v")
                else:
                    b.on("vector", lambda e, j=j: e.scalar_tensor_tensor(
                            out=bstage[:, 0:2176],
                            in0=(evbuf[:, 0:2176] if j % 2 == 0 else tbuf[:, 0:2176]),
                            scalar=topkb[:, j:j + 1], in1=bstage[:, 0:2176],
                            op0=ALU.mult, op1=ALU.add),
                         waits=[("gpsimddma", gdn)], inc="v")
            w_ok = b.c("v")
            transpose_to_arena(bstage, arA, NT, tt * 128, (2, 3), [("v", w_ok)],
                               nblk=NFC)
        wT_ok = b.c("v")

        for f in range(NFC):
            lw = b.dma("sync", evbuf[:, 0:1024],
                       bass.AP(xr_sp, f * 128 * NT, [[NT, 128], [1, NT]]),
                       waits=[("v", b.c("v"))])
            lw = b.dma("sync", evbuf[:, 1024:2048],
                       bass.AP(xi_sp, f * 128 * NT, [[NT, 128], [1, NT]]))
            rscol = 24 if f == 0 else (25 if f == 16 else 26)

            def yri(e, f=f, rscol=rscol):
                e.scalar_tensor_tensor(out=bstage[:, 0:1024], in0=evbuf[:, 0:1024],
                                       scalar=small[:, rscol:rscol + 1],
                                       in1=arA[:, f * NT:(f + 1) * NT],
                                       op0=ALU.mult, op1=ALU.mult)
                return e.scalar_tensor_tensor(out=bstage[:, 1024:2048],
                                              in0=evbuf[:, 1024:2048],
                                              scalar=small[:, rscol:rscol + 1],
                                              in1=arA[:, f * NT:(f + 1) * NT],
                                              op0=ALU.mult, op1=ALU.mult)
            b.on("vector", yri, waits=[("syncdma", lw)], inc="v")

            def ycp(e, f=f):
                e.tensor_copy(out=arB[:, f * NT:(f + 1) * NT], in_=bstage[:, 0:1024])
                return e.tensor_copy(out=arA[:, f * NT:(f + 1) * NT],
                                     in_=bstage[:, 1024:2048])
            b.on("vector", ycp, inc="v")
        y_ok = b.c("v")

        for ic in range(32):
            lwc = b.dma("scalar", tbuf[:, 0:2176],
                        bass.AP(icos_ch, ic * NF * 128,
                                [[128, 128], [128 * 128, NFC], [1, 128]]),
                        waits=[("t", b.c("t"))])
            lws = b.dma("scalar", tbuf[:, 2176:4352],
                        bass.AP(isin_ch, ic * NF * 128,
                                [[128, 128], [128 * 128, NFC], [1, 128]]))
            for th in range(2):
                bank = (ic % 2) * 2 + th
                n = 0
                for f in range(NFC):
                    for ri in range(2):
                        src = arB if ri == 0 else arA
                        wl = [("scalardma", lws), ("v", y_ok)]
                        if n == 0:
                            wl.append(("sevac", bank_free.get(("iv", bank), 0)))
                        b.on("tensor", lambda e, bank=bank, f=f, ri=ri, th=th, src=src:
                             e.matmul(ps[bank][:, :],
                                      tbuf[:, ri * 2176 + f * 128:
                                           ri * 2176 + (f + 1) * 128],
                                      src[:, f * NT + th * 512:f * NT + (th + 1) * 512],
                                      start=(f == 0 and ri == 0),
                                      stop=(f == NFC - 1 and ri == 1)),
                             waits=wl,
                             inc=("t" if (f == NFC - 1 and ri == 1) else None))
                        n += 1
            tdn = b.c("t")
            for th in range(2):
                bank = (ic % 2) * 2 + th
                slot = th
                b.on("scalar", lambda e, bank=bank, slot=slot: e.activation(
                        out=evbuf[:, 2048 + slot * 512:2048 + (slot + 1) * 512],
                        in_=ps[bank][:, :], func=AF.Copy),
                     waits=[("t", tdn), ev_slot_free[slot]], inc="sevac")
                sdn = b.c("sevac")
                bank_free[("iv", bank)] = sdn
                for j in range(4):
                    b.on("tensor", lambda e, j=j, th=th, slot=slot: e.transpose(
                            ps[4 + th][:, j * 64:(j + 1) * 64].bitcast(BF16),
                            evbuf[:, 2048 + slot * 512 + j * 128:
                                  2048 + slot * 512 + (j + 1) * 128], ident[:]),
                         waits=[("sevac", sdn)], inc=("t" if j == 3 else None))
                tdn2 = b.c("t")
                b.on("vector", lambda e, th=th, slot=slot: e.tensor_copy(
                        out=evbuf[:, 3072 + slot * 512:3072 + (slot + 1) * 512]
                        if False else evbuf[:, slot * 512:(slot + 1) * 512],
                        in_=ps[4 + th][:, 0:256].bitcast(BF16)),
                     waits=[("t", tdn2)], inc="v")
                vdn = b.c("v")
                for j in range(4):
                    b.dma("gpsimd",
                          bass.AP(att_sp, (th * 512 + j * 128) * D + ic * 128,
                                  [[D, 128], [1, 128]]),
                          evbuf[:, slot * 512 + j * 128:slot * 512 + (j + 1) * 128],
                          waits=[("v", vdn)])
                ev_slot_free[slot] = ("gpsimddma", b.c("gpsimddma"))
        yto_g = b.c("gpsimddma")
        for tt in range(8):
            lw = b.dma("sync", bstage[:], att_sp.ap()[tt * 128:(tt + 1) * 128, :],
                       waits=[("gpsimddma", yto_g), ("v", b.c("v"))])
            lw2 = b.dma("sync", stage2[:], res_b.ap()[tt * 128:(tt + 1) * 128, :])
            b.on("vector", lambda e: TT(e, stage[:], stage2[:], bstage[:], ALU.add),
                 waits=[("syncdma", lw2)], inc="v")
            vdn = b.c("v")
            lw = b.dma("sync", res_a.ap()[tt * 128:(tt + 1) * 128, :], stage[:],
                       waits=[("v", vdn)])
            b.on("vector", lambda e: e.nop(), waits=[("syncdma", lw)], inc="v")
        d = dbg_out("x3", (NT, D), F32)
        if d is not None:
            b.dma("sync", d[:, :], res_a.ap()[:, :],
                  waits=[("syncdma", b.c("syncdma"))])

    b.barrier()

    if "ffn2" in PHASES:
        ffn_phase(res_a, out_h, "f2")

    b.on("sync", lambda e: e.nop(), waits=[("syncdma", b.c("syncdma"))], inc="bar")
    for eng in ("tensor", "vector", "scalar", "gpsimd"):
        b.on(eng, lambda e: e.nop(), inc="bar")

    b.emit(ctx.enter_context(nc.Block()))
    ctx.close()
    return nc, dbg_aps


def make_consts(core_h):
    j = np.arange(D, dtype=np.float64)
    k = np.arange(NF, dtype=np.float64)
    ang = 2.0 * np.pi * np.outer(j, k) / 4096.0
    cosT = np.cos(ang)
    sinT = np.sin(ang)
    cosT[:, 2049:] = 0.0
    sinT[:, 2049:] = 0.0
    cos_rows = np.ascontiguousarray(cosT.astype(ml_dtypes.bfloat16))
    cos_ch = np.ascontiguousarray(cos_rows.reshape(D, NFC, 128).transpose(1, 0, 2))
    sin_ch = np.ascontiguousarray(
        sinT.astype(ml_dtypes.bfloat16).reshape(D, NFC, 128).transpose(1, 0, 2))
    i = np.arange(D, dtype=np.float64)
    angT = 2.0 * np.pi * np.outer(k, i) / 4096.0
    icos = np.cos(angT)
    isin = np.sin(angT)
    icos[2049:, :] = 0.0
    isin[2049:, :] = 0.0
    icos_ch = np.ascontiguousarray(
        icos.astype(ml_dtypes.bfloat16).reshape(NF, 32, 128).transpose(1, 0, 2))
    isin_ch = np.ascontiguousarray(
        isin.astype(ml_dtypes.bfloat16).reshape(NF, 32, 128).transpose(1, 0, 2))
    invf = 1.0 / (10000.0 ** (np.arange(0, HD, 2, dtype=np.float64) / HD))
    pos = np.arange(NKV, dtype=np.float64)
    fr = np.outer(invf, pos)
    rc = np.ascontiguousarray(
        np.concatenate([np.cos(fr), np.cos(fr)], 0).astype(np.float32))
    rs = np.ascontiguousarray(
        np.concatenate([-np.sin(fr), np.sin(fr)], 0).astype(np.float32))
    o = core_h * NT
    return dict(cos_ch=cos_ch, sin_ch=sin_ch, icos_ch=icos_ch, isin_ch=isin_ch,
                cos_rows=cos_rows, ropecos=rc, ropesin=rs,
                ropecos_own=np.ascontiguousarray(rc[:, o:o + NT]),
                ropesin_own=np.ascontiguousarray(rs[:, o:o + NT]))


def kernel(L0_attn_a_hidden_states, L0_attn_a_ln_w, L0_attn_a_ln_b,
           L0_attn_a_Wq, L0_attn_a_Wk, L0_attn_a_Wv, L0_attn_a_Wo,
           L0_ffn_up, L0_ffn_down, L0_ffn_perm, _dbg=()):
    x = np.ascontiguousarray(np.asarray(L0_attn_a_hidden_states, np.float32))
    wq = np.ascontiguousarray(np.asarray(L0_attn_a_Wq, np.float32))
    wk = np.ascontiguousarray(np.asarray(L0_attn_a_Wk, np.float32))
    wv = np.ascontiguousarray(np.asarray(L0_attn_a_Wv, np.float32))
    wo = np.ascontiguousarray(np.asarray(L0_attn_a_Wo, np.float32))
    nc, dbg_aps = build_program(dbg=_dbg)
    consts = [make_consts(0), make_consts(1)]
    in_maps = []
    for c in range(8):
        bi, h = c // 2, c % 2
        in_maps.append({
            "x_own": np.ascontiguousarray(x[bi, h * NT:(h + 1) * NT]),
            "gamma1": np.ascontiguousarray(np.asarray(L0_attn_a_ln_w, np.float32)[None, :]),
            "beta1": np.ascontiguousarray(np.asarray(L0_attn_a_ln_b, np.float32)[None, :]),
            "wq_sh": np.ascontiguousarray(wq[c * 512:(c + 1) * 512]),
            "wk_sh": np.ascontiguousarray(wk[c * 128:(c + 1) * 128]),
            "wv_sh": np.ascontiguousarray(wv[c * 128:(c + 1) * 128]),
            "wo_sh": np.ascontiguousarray(wo[c * 512:(c + 1) * 512]),
            "up_w": np.ascontiguousarray(np.asarray(L0_ffn_up, np.float32)),
            "down_w": np.ascontiguousarray(np.asarray(L0_ffn_down, np.float32)),
            "perm_in": np.ascontiguousarray(np.asarray(L0_ffn_perm, np.int32)),
            **consts[h],
        })
    res = run_bass_kernel_spmd(nc, in_maps, core_ids=list(range(8)))
    outs = np.empty((4, 2048, D), np.float32)
    for c in range(8):
        bi, h = c // 2, c % 2
        outs[bi, h * NT:(h + 1) * NT] = res.results[c]["out"]
    if _dbg:
        return outs, res.results
    return outs
